# version 16
# baseline (speedup 1.0000x reference)
import numpy as np
from contextlib import ExitStack
import concourse.bass as bass
import concourse.mybir as mybir
from concourse.bass_utils import run_bass_kernel_spmd

F32 = mybir.dt.float32
BF16 = mybir.dt.bfloat16
AF = mybir.ActivationFunctionType
ALU = mybir.AluOpType
AX = mybir.AxisListType.X

P = 128
D = 2048
DI = 3072
NKT = D // P
NCT = DI // P
W = 128
TOK = 2048
TT = W + TOK
CH1 = [(0, 128), (128, 512), (640, 512), (1152, 512), (1664, 512)]
NCH2 = 4
INV_D = 1.0 / D
EPS = 1e-5


class Plan:

    def __init__(self, nc):
        self.nc = nc
        self.steps = {k: [] for k in ("sync", "scalar", "vector", "tensor", "gpsimd")}
        self.sems = {}

    def sem(self, name, es):
        h = es.enter_context(self.nc.semaphore(name))
        self.sems[name] = {"h": h, "total": 0}

    def _resolve(self, waits):
        best = {}
        for (s, v) in waits:
            assert v <= self.sems[s]["total"], (s, v, self.sems[s]["total"])
            best[s] = max(best.get(s, 0), v)
        return [(self.sems[s]["h"], v) for s, v in best.items()]

    def add(self, engine, fn, waits=(), inc=None, amt=1):
        w = self._resolve(waits)
        if inc is not None:
            self.sems[inc]["total"] += amt
            self.steps[engine].append((fn, w, (self.sems[inc]["h"], amt)))
            return self.sems[inc]["total"]
        self.steps[engine].append((fn, w, None))
        return None

    def total(self, name):
        return self.sems[name]["total"]

    def emit(self, engine_name, eng):
        for fn, waits, inc in self.steps[engine_name]:
            for (h, v) in waits:
                eng.wait_ge(h, v)
            inst = fn(eng)
            if inc is not None:
                inst.then_inc(inc[0], inc[1])


def build_program(debug=False):
    nc = bass.Bass("TRN2", target_bir_lowering=False)

    x_d = nc.dram_tensor("x", [D, TT], F32, kind="ExternalInput").ap()
    whg_d = nc.dram_tensor("whg", [48, P, NKT * P], F32, kind="ExternalInput").ap()
    wout_d = nc.dram_tensor("wout", [16, P, NCT * P], F32, kind="ExternalInput").ap()
    r_d = nc.dram_tensor("rw", [P, NKT * 8], F32, kind="ExternalInput").ap()
    dw_d = nc.dram_tensor("dw", [P, NKT * P], F32, kind="ExternalInput").ap()
    uw_d = nc.dram_tensor("uw", [P, 16 * P], F32, kind="ExternalInput").ap()
    sel_d = nc.dram_tensor("sel8", [8, P], F32, kind="ExternalInput").ap()
    id_d = nc.dram_tensor("ident", [P, P], F32, kind="ExternalInput").ap()
    out_d = nc.dram_tensor("out", [D, TOK], F32, kind="ExternalOutput").ap()
    dbg_kind = "ExternalOutput" if debug else "Internal"
    l_d = nc.dram_tensor("l_scratch", [NCT, P, TOK], BF16, kind=dbg_kind).ap()
    wob_d = nc.dram_tensor("wob", [16, P, NCT * P], BF16, kind=dbg_kind).ap()
    taps = {}
    if debug:
        taps["t_xln"] = nc.dram_tensor("t_xln", [P, NKT * TT], BF16,
                                       kind="ExternalOutput").ap()
        taps["t_xnew"] = nc.dram_tensor("t_xnew", [NCH2, P, NKT * 512], F32,
                                        kind="ExternalOutput").ap()
        taps["t_ls"] = nc.dram_tensor("t_ls", [NCH2, 8, 512], F32,
                                      kind="ExternalOutput").ap()
        taps["t_gem"] = nc.dram_tensor("t_gem", [NCH2, 8, 512], F32,
                                       kind="ExternalOutput").ap()
        taps["t_asc"] = nc.dram_tensor("t_asc", [NCH2, P, 512], BF16,
                                       kind="ExternalOutput").ap()
        taps["t_crow"] = nc.dram_tensor("t_crow", [P, TT], F32,
                                        kind="ExternalOutput").ap()
        taps["t_vrow"] = nc.dram_tensor("t_vrow", [P, TT], F32,
                                        kind="ExternalOutput").ap()

    es = ExitStack()
    with es:
        def sb(name, shape, dt):
            return es.enter_context(nc.sbuf_tensor(name, shape, dt))

        pl = Plan(nc)
        for s in ("s_ld", "s_ld2", "s_g", "s_st", "s_pe", "s_dv", "s_ac", "s_pl",
                  "s_cst", "s_cst2", "s_x0", "s_x1", "s_wb", "s_wo", "s_lst",
                  "s_ws0", "s_ws1", "s_x2a", "s_x2b", "s_st0", "s_st1", "s_tap",
                  "s_wr0", "s_wr1", "s_wr2", "s_wr3", "s_wr4", "s_wr5", "s_wr6",
                  "s_wr7"):
            pl.sem(s, es)

        ident_sb = sb("ident_sb", [P, P], F32)
        sel_sb = sb("sel_sb", [8, P], F32)
        r_sb = sb("r_sb", [P, NKT * 8], BF16)
        d_sb = sb("d_sb", [P, NKT * P], BF16)
        u_sb = sb("u_sb", [P, 16 * P], BF16)
        ones_b = sb("ones_b", [P, 1], BF16)
        onesr_b = sb("onesr_b", [1, P], BF16)
        ones8f = sb("ones8f", [8, 1], F32)
        ones1x8 = sb("ones1x8", [1, 8], F32)
        ceps = sb("ceps", [1, 1], F32)
        row_mu = sb("row_mu", [1, 512], F32)
        row_ss = sb("row_ss", [1, 512], F32)
        row_mu2 = sb("row_mu2", [1, 512], F32)
        row_var = sb("row_var", [1, 512], F32)
        row_lnv = sb("row_lnv", [1, 512], F32)
        row_rstd = sb("row_rstd", [1, 512], F32)
        row_q = sb("row_q", [1, 512], F32)
        row_r_bf = sb("row_r_bf", [1, 512], BF16)
        zrow = sb("zrow", [1, 512], F32)
        settle_b = sb("settle_b", [P, 16], F32)
        zeros32 = sb("zeros32", [P, 32], BF16)
        row_q_bf = sb("row_q_bf", [1, 512], BF16)

        pl.add("sync", lambda e: e.dma_start(out=ident_sb[:], in_=id_d[:]), inc="s_cst", amt=16)
        pl.add("sync", lambda e: e.dma_start(out=sel_sb[:], in_=sel_d[:]), inc="s_cst", amt=16)
        pl.add("gpsimd", lambda e: e.dma_start(out=r_sb[:], in_=r_d[:]), inc="s_cst2", amt=16)
        pl.add("gpsimd", lambda e: e.dma_start(out=d_sb[:], in_=dw_d[:]), inc="s_cst2", amt=16)
        pl.add("gpsimd", lambda e: e.dma_start(out=u_sb[:], in_=uw_d[:]), inc="s_cst2", amt=16)
        pl.add("vector", lambda e: e.memset(ones_b[:], 1.0))
        pl.add("vector", lambda e: e.memset(onesr_b[:], 1.0))
        pl.add("vector", lambda e: e.memset(ones8f[:], 1.0))
        pl.add("vector", lambda e: e.memset(ones1x8[:], 1.0))
        pl.add("vector", lambda e: e.memset(zrow[:], 0.0))
        pl.add("vector", lambda e: e.memset(zeros32[:], 0.0))
        pl.add("vector", lambda e: e.memset(settle_b[:], 0.0))
        pl.add("vector", lambda e: e.memset(ceps[:], EPS), inc="s_dv")
        consts_dv = ("s_dv", pl.total("s_dv"))
        consts_ld = ("s_cst", 32)
        consts_g = ("s_cst2", 48)

        p1 = ExitStack()

        def sb1(name, shape, dt):
            return p1.enter_context(nc.sbuf_tensor(name, shape, dt))

        def ps1(name, shape):
            return p1.enter_context(nc.psum_tensor(name, shape, F32))

        x_ln = sb1("x_ln", [P, NKT * TT], BF16)
        wbuf = [sb1("wbuf0", [P, NKT * P], F32)]
        xb = sb1("xb", [P, NKT * 512], BF16)
        sq_t = [sb1(f"sq_t{i}", [P, 512], BF16) for i in range(2)]
        xf = [sb1(f"xf{i}", [P, 512], F32) for i in range(2)]
        wbf = [sb1(f"wbf{i}", [P, NKT * P], BF16) for i in range(4)]
        rb_sb = [sb1(f"rb_sb{i}", [P, 512], BF16) for i in range(2)]
        qb_sb = [sb1(f"qb_sb{i}", [P, 512], BF16) for i in range(2)]
        xt_tmp = [sb1(f"xt_tmp{i}", [P, 512], BF16) for i in range(2)]
        c_row = [sb1(f"c_row{i}", [P, TT], F32) for i in range(2)]
        v_row = [sb1(f"v_row{i}", [P, TT], F32) for i in range(2)]
        l_row = sb1("l_row", [P, TT], F32)
        z_t = [sb1(f"z_t{i}", [P, 512], BF16) for i in range(2)]
        sp_t = [sb1(f"sp_t{i}", [P, 512], BF16) for i in range(2)]
        t_t = [sb1(f"t_t{i}", [P, 512], F32) for i in range(2)]

        p_big = [ps1(f"p_big{i}", [P, 512]) for i in range(4)]
        p_cmb = [ps1(f"p_cmb{i}", [P, 512]) for i in range(4)]

        bank_free = {i: [] for i in range(4)}

        xf_used = {}
        xb_mm_pe = {}
        sq_mm_pe = {}
        ln1_state = {"prev_xln_dv": 0}
        xln_chunk_dv = {}
        cmb_free = {i: [] for i in range(4)}

        def emit_ln1(ci):
            t0, ln = CH1[ci]
            par = ci % 2
            sS, sSS = p_cmb[2 * par], p_cmb[2 * par + 1]
            bR, bQ = p_cmb[2 * par], p_cmb[2 * par + 1]
            prev_xln_dv = ln1_state["prev_xln_dv"]
            smw = cmb_free[2 * par] + cmb_free[2 * par + 1]
            cmb_free[2 * par] = []
            cmb_free[2 * par + 1] = []
            for kt in range(NKT):
                xbuf = xf[kt % 2]
                xsem = f"s_x{kt % 2}"
                pl.add("sync",
                       (lambda e, xb_=xbuf, kt_=kt, t0_=t0, ln_=ln:
                        e.dma_start(out=xb_[:, 0:ln_],
                                    in_=x_d[kt_ * P:(kt_ + 1) * P, t0_:t0_ + ln_])),
                       waits=xf_used.get(kt % 2, []), inc=xsem, amt=16)
                ldv = (xsem, pl.total(xsem))
                cwts = [ldv, ("s_dv", prev_xln_dv)]
                if kt in xb_mm_pe:
                    cwts.append(("s_pe", xb_mm_pe[kt]))
                cac = pl.add("scalar",
                             (lambda e, xb_=xbuf, kt_=kt, ln_=ln:
                              e.copy(xb[:, kt_ * 512:kt_ * 512 + ln_],
                                     xb_[:, 0:ln_])),
                             waits=cwts, inc="s_ac")
                swts = [ldv]
                if kt - 2 in sq_mm_pe:
                    swts.append(("s_pe", sq_mm_pe[kt - 2]))
                sdv = pl.add("gpsimd",
                             (lambda e, xb_=xbuf, s_=sq_t[kt % 2], ln_=ln:
                              e.tensor_tensor(out=s_[:, 0:ln_], in0=xb_[:, 0:ln_],
                                              in1=xb_[:, 0:ln_], op=ALU.mult)),
                             waits=swts, inc="s_pl")
                xf_used[kt % 2] = [("s_ac", cac), ("s_pl", sdv)]
                pl.add("tensor",
                       (lambda e, kt_=kt, ln_=ln, s_=sS:
                        e.matmul(s_[0:1, 0:ln_], ones_b[:],
                                 xb[:, kt_ * 512:kt_ * 512 + ln_],
                                 start=(kt_ == 0), stop=(kt_ == NKT - 1))),
                       waits=[("s_ac", cac), consts_dv] + (smw if kt == 0 else []))
                pe_v = pl.add("tensor",
                              (lambda e, kt_=kt, ln_=ln, s_=sSS, q_=sq_t[kt % 2]:
                               e.matmul(s_[0:1, 0:ln_], ones_b[:], q_[:, 0:ln_],
                                        start=(kt_ == 0), stop=(kt_ == NKT - 1))),
                              waits=[("s_pl", sdv)], inc="s_pe")
                sq_mm_pe[kt] = pe_v
                xb_mm_pe[kt] = pe_v
            stats_pe = pl.total("s_pe")
            pl.add("vector",
                   (lambda e, s_=sS, ln_=ln:
                    e.scalar_tensor_tensor(out=row_mu[:, 0:ln_], in0=s_[0:1, 0:ln_],
                                           scalar=INV_D, in1=zrow[:, 0:ln_],
                                           op0=ALU.mult, op1=ALU.add)),
                   waits=[("s_pe", stats_pe)])
            pl.add("vector",
                   (lambda e, s_=sSS, ln_=ln:
                    e.scalar_tensor_tensor(out=row_ss[:, 0:ln_], in0=s_[0:1, 0:ln_],
                                           scalar=INV_D, in1=zrow[:, 0:ln_],
                                           op0=ALU.mult, op1=ALU.add)))
            pl.add("vector",
                   (lambda e, ln_=ln:
                    e.tensor_tensor(out=row_mu2[:, 0:ln_], in0=row_mu[:, 0:ln_],
                                    in1=row_mu[:, 0:ln_], op=ALU.mult)))
            pl.add("vector", lambda e: e.drain())
            v_sm = pl.add("vector",
                          (lambda e, ln_=ln:
                           e.tensor_tensor(out=row_var[:, 0:ln_], in0=row_ss[:, 0:ln_],
                                           in1=row_mu2[:, 0:ln_], op=ALU.subtract)),
                          inc="s_dv")
            pl.add("scalar",
                   (lambda e, ln_=ln:
                    e.activation(out=row_lnv[:, 0:ln_], in_=row_var[:, 0:ln_],
                                 func=AF.Ln, bias=ceps[0:1, 0:1])),
                   waits=[("s_dv", v_sm)])
            a_r = pl.add("scalar",
                         (lambda e, ln_=ln:
                          e.activation(out=row_rstd[:, 0:ln_], in_=row_lnv[:, 0:ln_],
                                       func=AF.Exp, scale=-0.5)),
                         inc="s_ac")
            pl.add("vector",
                   (lambda e, ln_=ln:
                    e.tensor_tensor(out=row_q[:, 0:ln_], in0=row_mu[:, 0:ln_],
                                    in1=row_rstd[:, 0:ln_], op=ALU.mult)),
                   waits=[("s_ac", a_r)])
            pl.add("vector", lambda e: e.drain())
            pl.add("vector",
                   (lambda e, ln_=ln:
                    e.tensor_tensor(out=row_r_bf[:, 0:ln_], in0=row_rstd[:, 0:ln_],
                                    in1=zrow[:, 0:ln_], op=ALU.add)))
            pl.add("vector",
                   (lambda e, ln_=ln:
                    e.tensor_tensor(out=row_q_bf[:, 0:ln_], in0=row_q[:, 0:ln_],
                                    in1=zrow[:, 0:ln_], op=ALU.add)))
            v_rows = pl.add("vector",
                            (lambda e: e.tensor_tensor(out=settle_b[:], in0=settle_b[:],
                                                       in1=settle_b[:], op=ALU.add)),
                            inc="s_dv")
            wts = [("s_dv", v_rows)]
            pl.add("tensor",
                   (lambda e, b_=bR, ln_=ln:
                    e.matmul(b_[:, 0:ln_], onesr_b[:], row_r_bf[:, 0:ln_],
                             start=True, stop=True)),
                   waits=wts)
            bc_pe = pl.add("tensor",
                           (lambda e, b_=bQ, ln_=ln:
                            e.matmul(b_[:, 0:ln_], onesr_b[:], row_q_bf[:, 0:ln_],
                                     start=True, stop=True)),
                           inc="s_pe")
            pl.add("vector",
                   (lambda e, b_=bR, p_=par, ln_=ln:
                    e.tensor_copy(rb_sb[p_][:, 0:ln_], b_[:, 0:ln_])),
                   waits=[("s_pe", bc_pe)])
            v_bc = pl.add("vector",
                          (lambda e, b_=bQ, p_=par, ln_=ln:
                           e.tensor_copy(qb_sb[p_][:, 0:ln_], b_[:, 0:ln_])),
                          inc="s_dv")
            cmb_free[2 * par].append(("s_dv", v_bc))
            cmb_free[2 * par + 1].append(("s_dv", v_bc))
            def xmult(kt, par_=par, ln_=ln):
                pl.add("vector",
                       (lambda e, kt_=kt:
                        e.tensor_tensor(out=xt_tmp[kt_ % 2][:, 0:ln_],
                                        in0=xb[:, kt_ * 512:kt_ * 512 + ln_],
                                        in1=rb_sb[par_][:, 0:ln_], op=ALU.mult)))

            def xsub(kt, par_=par, t0_=t0, ln_=ln):
                pl.add("vector",
                       (lambda e, kt_=kt:
                        e.tensor_tensor(out=x_ln[:, kt_ * TT + t0_:kt_ * TT + t0_ + ln_],
                                        in0=xt_tmp[kt_ % 2][:, 0:ln_],
                                        in1=qb_sb[par_][:, 0:ln_], op=ALU.subtract)),
                       inc=("s_dv" if kt == NKT - 1 else None))

            xmult(0)
            for kt in range(NKT):
                if kt + 1 < NKT:
                    xmult(kt + 1)
                xsub(kt)
            pl.add("vector",
                   (lambda e: e.tensor_tensor(out=settle_b[:], in0=settle_b[:],
                                              in1=settle_b[:], op=ALU.add)),
                   inc="s_dv")
            ln1_state["prev_xln_dv"] = pl.total("s_dv")
            xln_chunk_dv[ci] = ln1_state["prev_xln_dv"]

        emit_ln1(0)
        emit_ln1(1)

        wbf_free = {i: [] for i in range(4)}
        wbf_cast = {}
        wbuf_cast = {}

        def load_pair_weights(pp):
            for half, m in enumerate((24 + pp, pp)):
                slot = (2 * pp + half) % 4
                fslot = 0
                lw = [("s_ac", wbuf_cast[fslot])] if fslot in wbuf_cast else []
                pl.add("sync",
                       (lambda e, m_=m, f_=fslot:
                        e.dma_start(out=wbuf[f_][:], in_=whg_d[m_, :, :])),
                       waits=lw, inc="s_wb", amt=16)
                ldv = pl.total("s_wb")
                ac = pl.add("scalar",
                            (lambda e, f_=fslot, s_=slot:
                             e.copy(wbf[s_][:], wbuf[f_][:])),
                            waits=[("s_wb", ldv)] + wbf_free[slot], inc="s_ac")
                wbf_free[slot] = []
                wbuf_cast[fslot] = ac
                wbf_cast[slot] = ac

        wout_store_g = {}

        def emit_wout_slab(mi):
            pl.add("gpsimd",
                   (lambda e, mi_=mi:
                    e.dma_start(out=wob_d[mi_, :, :], in_=wout_d[mi_, :, :])),
                   inc="s_wo", amt=16)
            wout_store_g[mi] = pl.total("s_wo")

        row_free = {0: [], 1: []}
        lrow_free = []
        scan_dv = {}
        crow_extra = {0: [], 1: []}
        ew_v_dv = {}
        ei = 0

        pair_mm_last = {}

        def emit_pair_chunk(pr, ci):
            nonlocal ei
            t0, ln = CH1[ci]
            rpar = pr % 2
            gate_wb = (2 * pr) % 4
            hid_wb = (2 * pr + 1) % 4
            crow, vrow = c_row[rpar], v_row[rpar]
            bgi = (2 * ei) % 4
            bhi = (2 * ei + 1) % 4
            bg, bh = p_big[bgi], p_big[bhi]
            epar = ei % 2
            wts = list(bank_free[bgi])
            if ci == 0:
                wts.append(("s_ac", max(wbf_cast[gate_wb], wbf_cast[hid_wb])))
            if pr <= 1:
                wts.append(("s_dv", xln_chunk_dv[ci]))
            bank_free[bgi] = []
            for kt in range(NKT):
                pl.add("tensor",
                       (lambda e, kt_=kt, t0_=t0, ln_=ln, b_=bg, w_=gate_wb:
                        e.matmul(b_[:, 0:ln_], wbf[w_][:, kt_ * P:(kt_ + 1) * P],
                                 x_ln[:, kt_ * TT + t0_:kt_ * TT + t0_ + ln_],
                                 start=(kt_ == 0), stop=(kt_ == NKT - 1))),
                       waits=(wts if kt == 0 else []),
                       inc=("s_pe" if kt == NKT - 1 else None))
            g_pe = pl.total("s_pe")
            wts = bank_free[bhi]
            bank_free[bhi] = []
            for kt in range(NKT):
                pl.add("tensor",
                       (lambda e, kt_=kt, t0_=t0, ln_=ln, b_=bh, w_=hid_wb:
                        e.matmul(b_[:, 0:ln_], wbf[w_][:, kt_ * P:(kt_ + 1) * P],
                                 x_ln[:, kt_ * TT + t0_:kt_ * TT + t0_ + ln_],
                                 start=(kt_ == 0), stop=(kt_ == NKT - 1))),
                       waits=(wts if kt == 0 else []),
                       inc=("s_pe" if kt == NKT - 1 else None))
            h_pe = pl.total("s_pe")
            pair_mm_last[pr] = h_pe
            cwts = [("s_pe", g_pe)]
            if ci == 0 and rpar in scan_dv:
                cwts.append(("s_dv", scan_dv[rpar]))
                cwts.extend(crow_extra[rpar])
            pl.add("scalar",
                   (lambda e, b_=bg, t0_=t0, ln_=ln, cr_=crow:
                    e.activation(out=cr_[:, t0_:t0_ + ln_], in_=b_[:, 0:ln_],
                                 func=AF.Sigmoid, scale=-1.0)),
                   waits=cwts)
            zwts = []
            if ei >= 2:
                zwts = [ew_v_dv[ei - 2]]
            pl.add("scalar",
                   (lambda e, b_=bg, ln_=ln, ep_=epar:
                    e.activation(out=z_t[ep_][:, 0:ln_], in_=b_[:, 0:ln_],
                                 func=AF.Sigmoid)),
                   waits=zwts, inc="s_ac")
            bank_free[bgi].append(("s_ac", pl.total("s_ac")))
            a_sp = pl.add("scalar",
                          (lambda e, b_=bh, ln_=ln, ep_=epar:
                           e.activation(out=sp_t[ep_][:, 0:ln_], in_=b_[:, 0:ln_],
                                        func=AF.Sigmoid)),
                          waits=[("s_pe", h_pe)], inc="s_ac")
            v_t1 = pl.add("vector",
                          (lambda e, b_=bh, ln_=ln, ep_=epar:
                           e.scalar_tensor_tensor(out=t_t[ep_][:, 0:ln_],
                                                  in0=b_[:, 0:ln_], scalar=0.5,
                                                  in1=sp_t[ep_][:, 0:ln_],
                                                  op0=ALU.add, op1=ALU.max)),
                          waits=[("s_ac", a_sp)], inc="s_dv")
            bank_free[bhi].append(("s_dv", v_t1))
            v_dv = pl.add("gpsimd",
                          (lambda e, t0_=t0, ln_=ln, vr_=vrow, ep_=epar:
                           e.tensor_tensor(out=vr_[:, t0_:t0_ + ln_],
                                           in0=z_t[ep_][:, 0:ln_],
                                           in1=t_t[ep_][:, 0:ln_], op=ALU.mult)),
                          waits=[("s_dv", v_t1)], inc="s_pl")
            ew_v_dv[ei] = ("s_pl", v_dv)
            ei += 1

        def emit_pair_scan(pr):
            nonlocal lrow_free
            rpar = pr % 2
            gate_wb = (2 * pr) % 4
            hid_wb = (2 * pr + 1) % 4
            crow, vrow = c_row[rpar], v_row[rpar]
            wait_rows = row_free[rpar]
            row_free[rpar] = []
            c_ac = pl.total("s_ac")
            c_pl = pl.total("s_pl")
            sdv = pl.add("vector",
                         (lambda e, cr_=crow, vr_=vrow:
                          e.tensor_tensor_scan(out=l_row[:], data0=cr_[:], data1=vr_[:],
                                               initial=0.0, op0=ALU.mult, op1=ALU.add)),
                         waits=[("s_ac", c_ac), ("s_pl", c_pl)] + wait_rows + lrow_free,
                         inc="s_dv")
            lrow_free = []
            scan_dv[rpar] = sdv
            pl.add("gpsimd",
                   (lambda e, pr_=pr:
                    e.dma_start(out=l_d[pr_, :, :], in_=l_row[:, W:TT])),
                   waits=[("s_dv", sdv)], inc="s_lst", amt=16)
            lrow_free = [("s_lst", pl.total("s_lst"))]
            row_free[rpar] = lrow_free
            if debug and pr == 0:
                pl.add("sync", (lambda e, cr_=crow: e.dma_start(out=taps["t_crow"][:],
                                                                in_=cr_[:])),
                       waits=[("s_dv", sdv)], inc="s_tap", amt=16)
                pl.add("sync", (lambda e, vr_=vrow: e.dma_start(out=taps["t_vrow"][:],
                                                                in_=vr_[:])),
                       waits=[("s_dv", sdv)], inc="s_tap", amt=16)
                crow_extra[0] = [("s_tap", pl.total("s_tap"))]
            wbf_free[gate_wb].append(("s_pe", pair_mm_last[pr]))
            wbf_free[hid_wb].append(("s_pe", pair_mm_last[pr]))

        load_pair_weights(0)
        load_pair_weights(1)
        for ci in (0, 1):
            emit_pair_chunk(0, ci)
            emit_pair_chunk(1, ci)
        for ci in (2, 3, 4):
            emit_ln1(ci)
            emit_pair_chunk(0, ci)
            emit_pair_chunk(1, ci)
        emit_pair_scan(0)
        emit_pair_scan(1)
        emit_wout_slab(0)
        emit_wout_slab(1)
        load_pair_weights(2)
        for pr in range(2, NCT):
            if pr < 16:
                emit_wout_slab(pr)
            for ci in range(5):
                if ci == 1 and pr + 1 < NCT:
                    load_pair_weights(pr + 1)
                emit_pair_chunk(pr, ci)
            emit_pair_scan(pr)

        if debug:
            pl.add("sync", (lambda e: e.dma_start(out=taps["t_xln"][:], in_=x_ln[:])),
                   waits=[("s_dv", xln_chunk_dv[4])], inc="s_tap", amt=16)
        P1 = {s: pl.total(s) for s in ("s_pe", "s_dv", "s_ac", "s_pl", "s_lst",
                                       "s_wo", "s_cst2")}
        p1.close()

        p2 = ExitStack()

        def sb2(name, shape, dt):
            return p2.enter_context(nc.sbuf_tensor(name, shape, dt))

        def ps2(name, shape):
            return p2.enter_context(nc.psum_tensor(name, shape, F32))

        wob_res = sb2("wob_res", [P, 8 * NCT * P], BF16)
        wob_st = [sb2(f"wob_st{i}", [P, NCT * P], BF16) for i in range(2)]
        l_sb = sb2("l_sb", [P, NCT * 512], BF16)
        x_new = [sb2(f"x_new{i}", [P, NKT * 512], F32) for i in range(2)]
        xf2 = [sb2(f"xf2_{i}", [P, 512], F32) for i in range(2)]
        xb2 = [sb2(f"xb2_{i}", [P, 512], BF16) for i in range(2)]
        sq2 = [sb2(f"sq2_{i}", [P, 512], BF16) for i in range(2)]
        yb = [sb2(f"yb_{i}", [P, 512], BF16) for i in range(2)]
        rb2 = sb2("rb2", [P, 512], BF16)
        qb2 = sb2("qb2", [P, 512], BF16)
        ls_sb = sb2("ls_sb", [8, 512], F32)
        lg = sb2("lg", [P, 32], F32)
        masked = sb2("masked", [P, 32], BF16)
        cds = [sb2(f"cds{i}", [P, 32], BF16) for i in range(7)]
        tsc = [sb2(f"tsc{i}", [P, 32], BF16) for i in range(5)]
        msk = sb2("msk", [P, 32], F32)
        e_em = sb2("e_em", [8, 512], F32)
        mk_em = sb2("mk_em", [8, 512], F32)
        s_row8 = sb2("s_row8", [1, 512], F32)
        rs_row8 = sb2("rs_row8", [1, 512], F32)
        g_em = sb2("g_em", [8, 512], F32)
        eneg = sb2("eneg", [P, 512], F32)
        a_sc = sb2("a_sc", [P, 512], BF16)

        p2_big = [ps2(f"p2_big{i}", [P, 512]) for i in range(4)]
        p2_sm = [ps2(f"p2_sm{i}", [1, 512]) for i in range(2)]
        p2_l8 = ps2("p2_l8", [8, 512])
        p2_tp = ps2("p2_tp", [P, 32])

        for hlf in range(2):
            pl.add("sync",
                   (lambda e, h_=hlf:
                    e.dma_start(
                        out=l_sb[:].rearrange("p (ct t) -> p ct t", ct=NCT)
                            [:, h_ * 12:(h_ + 1) * 12, :],
                        in_=l_d[h_ * 12:(h_ + 1) * 12, :, 0:512]
                            .transpose([1, 0, 2]))),
                   waits=([("s_lst", P1["s_lst"]), ("s_pe", P1["s_pe"])]
                          if hlf == 0 else []),
                   inc="s_ld2", amt=16)
        wob_slab_ld = {}
        for mi in range(8):
            pl.add("sync",
                   (lambda e, mi_=mi:
                    e.dma_start(out=wob_res[:, mi_ * NCT * P:(mi_ + 1) * NCT * P],
                                in_=wob_d[mi_, :, :])),
                   waits=([("s_pe", P1["s_pe"]), ("s_ac", P1["s_ac"]),
                           ("s_wo", wout_store_g[15])] if mi == 0 else []),
                   inc=f"s_wr{mi}", amt=16)
            wob_slab_ld[mi] = (f"s_wr{mi}", 16)

        bank01_free = {0: [], 1: []}
        bank23_free = {2: [], 3: []}
        sm2_free = []
        l8_free = []
        tp_free = []
        x_used = {}
        x_evac_dv = {}
        x_cast_ac = {}
        prev_store_st = {}
        y_used = {}
        y_dv = {}
        stats_mm_pe = {}
        ln2_end_pe = 0
        ls_guard = []
        msk_guard = []
        asc_guard = []
        eneg_guard = []
        wob_st_used = {}
        wob_st_ld = {}
        om_last_pe = {}
        rows_dv = {}
        bc_pe2 = {}
        v_bc_dv = {}

        def emit_lload(c, wts, queue="gpsimd"):
            for hlf in range(2):
                pl.add(queue,
                       (lambda e, c_=c, h_=hlf:
                        e.dma_start(
                            out=l_sb[:].rearrange("p (ct t) -> p ct t", ct=NCT)
                                [:, h_ * 12:(h_ + 1) * 12, :],
                            in_=l_d[h_ * 12:(h_ + 1) * 12, :, c_ * 512:(c_ + 1) * 512]
                                .transpose([1, 0, 2]))),
                       waits=(wts if hlf == 0 else []), inc="s_ld2", amt=16)
            return pl.total("s_ld2")

        def emit_bcast(c):
            wts = [("s_dv", rows_dv[c])] + bank23_free[2] + bank23_free[3]
            bank23_free[2] = []
            bank23_free[3] = []
            pl.add("tensor",
                   (lambda e: e.matmul(p2_big[2][:], onesr_b[:], row_r_bf[:, 0:512],
                                       start=True, stop=True)),
                   waits=wts)
            bc_pe2[c] = pl.add("tensor",
                               (lambda e: e.matmul(p2_big[3][:], onesr_b[:],
                                                   row_q_bf[:, 0:512],
                                                   start=True, stop=True)),
                               inc="s_pe")

        def emit_bcevac(c):
            pl.add("vector",
                   (lambda e: e.tensor_copy(rb2[:], p2_big[2][:])),
                   waits=[("s_pe", bc_pe2[c])])
            v = pl.add("vector",
                       (lambda e: e.tensor_copy(qb2[:], p2_big[3][:])),
                       inc="s_dv")
            v_bc_dv[c] = v
            bank23_free[2].append(("s_dv", v))
            bank23_free[3].append(("s_dv", v))

        def emit_ypair(c, kt):
            par = c % 2
            ywts = [("s_dv", v_bc_dv[c])] if kt == 0 else (
                [("s_pe", y_used[kt % 2])] if (kt % 2) in y_used else [])
            pl.add("vector",
                   (lambda e, kt_=kt, y_=yb[kt % 2], xn_=x_new[par]:
                    e.tensor_tensor(out=y_[:],
                                    in0=xn_[:, kt_ * 512:(kt_ + 1) * 512],
                                    in1=rb2[:], op=ALU.mult)),
                   waits=ywts)
            y_dv[kt] = pl.add("vector",
                              (lambda e, kt_=kt, y_=yb[kt % 2]:
                               e.tensor_tensor(out=y_[:], in0=y_[:],
                                               in1=qb2[:], op=ALU.subtract)),
                              inc="s_dv")

        router_pe_d = {}

        def emit_ylogits(c, kt):
            nonlocal l8_free
            wts_l8 = l8_free if kt == 0 else []
            if kt == 0:
                l8_free = []
            wts_h = bank23_free[2] if kt == 0 else []
            if kt == 0:
                bank23_free[2] = []
            pl.add("tensor",
                   (lambda e, kt_=kt, y_=yb[kt % 2]:
                    e.matmul(p2_l8[:], r_sb[:, kt_ * 8:(kt_ + 1) * 8], y_[:],
                             start=(kt_ == 0), stop=(kt_ == NKT - 1))),
                   waits=[("s_dv", y_dv[kt])] + wts_l8
                         + ([consts_g] if (c == 0 and kt == 0) else []))
            y_used[kt % 2] = pl.add("tensor",
                                    (lambda e, kt_=kt, y_=yb[kt % 2]:
                                     e.matmul(p2_big[2][:],
                                              d_sb[:, kt_ * P:(kt_ + 1) * P], y_[:],
                                              start=(kt_ == 0),
                                              stop=(kt_ == NKT - 1))),
                                    waits=wts_h, inc="s_pe")
            if kt == NKT - 1:
                router_pe_d[c] = pl.total("s_pe")

        def emit_outmm(c, bc_of=None):
            par = c % 2
            l_ld = pl.total("s_ld2")
            for s in (8, 9):
                lwts = list(wob_st_used.get(s % 2, []))
                if c == 0:
                    lwts.append(("s_wo", wout_store_g[15]))
                pl.add("sync",
                       (lambda e, s_=s, b2_=wob_st[s % 2]:
                        e.dma_start(out=b2_[:], in_=wob_d[s_, :, :])),
                       waits=lwts, inc=f"s_ws{s % 2}", amt=16)
                wob_st_ld[s] = (f"s_ws{s % 2}", pl.total(f"s_ws{s % 2}"))
            for m in range(NKT):
                b_i = m % 2
                b = p2_big[b_i]
                xw = [("s_dv", x_used[m % 2])] if (m % 2) in x_used else [
                    ("s_lst", P1["s_lst"]), ("s_ac", P1["s_ac"])]
                x2sem = "s_x2a" if m % 2 == 0 else "s_x2b"
                pl.add("sync",
                       (lambda e, m_=m, c_=c, x_=xf2[m % 2]:
                        e.dma_start(out=x_[:],
                                    in_=x_d[m_ * P:(m_ + 1) * P,
                                            W + c_ * 512:W + c_ * 512 + 512])),
                       waits=xw, inc=x2sem, amt=16)
                x_ld = (x2sem, pl.total(x2sem))
                wts = bank01_free[b_i] + ([("s_ld2", l_ld)] if m == 0 else [])
                bank01_free[b_i] = []
                if c == 0 and m < 8:
                    wts.append(wob_slab_ld[m])
                if m >= 8:
                    wts.append(wob_st_ld[m])
                for ct in range(NCT):
                    if m < 8:
                        lhs = (lambda e, m_=m, ct_=ct:
                               wob_res[:, m_ * NCT * P + ct_ * P:
                                       m_ * NCT * P + (ct_ + 1) * P])
                    else:
                        lhs = (lambda e, m_=m, ct_=ct, b2_=wob_st[m % 2]:
                               b2_[:, ct_ * P:(ct_ + 1) * P])
                    pl.add("tensor",
                           (lambda e, ct_=ct, b_=b, lhs_=lhs:
                            e.matmul(b_[:], lhs_(e),
                                     l_sb[:, ct_ * 512:(ct_ + 1) * 512],
                                     start=(ct_ == 0), stop=(ct_ == NCT - 1))),
                           waits=(wts if ct == 0 else []),
                           inc=("s_pe" if ct == NCT - 1 else None))
                om_pe = pl.total("s_pe")
                if m >= 8:
                    wob_st_used[m % 2] = [("s_pe", om_pe)]
                s = m + 2
                if 8 <= s <= 15 and m >= 6:
                    lwts = list(wob_st_used.get(s % 2, []))
                    if c == 0:
                        lwts.append(("s_wo", wout_store_g[15]))
                    pl.add("sync",
                           (lambda e, s_=s, b2_=wob_st[s % 2]:
                            e.dma_start(out=b2_[:], in_=wob_d[s_, :, :])),
                           waits=lwts, inc=f"s_ws{s % 2}", amt=16)
                    wob_st_ld[s] = (f"s_ws{s % 2}", pl.total(f"s_ws{s % 2}"))
                ewts = [("s_pe", om_pe), x_ld]
                if par in prev_store_st:
                    ewts.append(prev_store_st[par])
                if (par, m) in x_cast_ac:
                    ewts.append(("s_ac", x_cast_ac[(par, m)]))
                v_e = pl.add("vector",
                             (lambda e, m_=m, b_=b, x_=xf2[m % 2], xn_=x_new[par]:
                              e.tensor_tensor(out=xn_[:, m_ * 512:(m_ + 1) * 512],
                                              in0=b_[:], in1=x_[:], op=ALU.add)),
                             waits=ewts, inc="s_dv")
                x_used[m % 2] = v_e
                x_evac_dv[(par, m)] = v_e
                bank01_free[b_i].append(("s_dv", v_e))
                if bc_of is not None:
                    if m == 2:
                        emit_bcast(bc_of)
                        emit_bcevac(bc_of)
                        emit_ypair(bc_of, 0)
                    elif m == 3:
                        emit_ypair(bc_of, 1)
                    elif m >= 4:
                        emit_ylogits(bc_of, m - 4)
                        emit_ypair(bc_of, m - 2)
            if bc_of is not None:
                emit_ylogits(bc_of, 12)
                emit_ypair(bc_of, 14)
                emit_ylogits(bc_of, 13)
                emit_ypair(bc_of, 15)
                emit_ylogits(bc_of, 14)
                emit_ylogits(bc_of, 15)
            om_last_pe[c] = pl.total("s_pe")
            if debug:
                pl.add("sync", (lambda e, c_=c, xn_=x_new[par]:
                                e.dma_start(out=taps["t_xnew"][c_, :, :], in_=xn_[:])),
                       waits=[("s_dv", pl.total("s_dv"))], inc="s_tap", amt=16)

        def emit_stats(c):
            nonlocal_state = None
            par = c % 2
            for m in range(NKT):
                cwts = [("s_dv", x_evac_dv[(par, m)])]
                if m >= 2:
                    cwts.append(("s_pe", stats_mm_pe[m - 2]))
                elif c > 0:
                    cwts.append(("s_pe", stats_mm_pe[NKT - 1]))
                cac2 = pl.add("scalar",
                              (lambda e, m_=m, x2_=xb2[m % 2], xn_=x_new[par]:
                               e.copy(x2_[:], xn_[:, m_ * 512:(m_ + 1) * 512])),
                              waits=cwts, inc="s_ac")
                x_cast_ac[(par, m)] = cac2
                swts = [("s_ac", cac2)]
                if m >= 2:
                    swts.append(("s_pe", stats_mm_pe[m - 2]))
                sdv2 = pl.add("vector",
                              (lambda e, x2_=xb2[m % 2], s_=sq2[m % 2]:
                               e.tensor_tensor(out=s_[:], in0=x2_[:], in1=x2_[:],
                                               op=ALU.mult)),
                              waits=swts, inc="s_dv")
                mwts = [("s_ac", cac2)] + (sm2_free if m == 0 else [])
                pl.add("tensor",
                       (lambda e, m_=m, x2_=xb2[m % 2]:
                        e.matmul(p2_sm[0][0:1, :], ones_b[:], x2_[:],
                                 start=(m_ == 0), stop=(m_ == NKT - 1))),
                       waits=mwts)
                stats_mm_pe[m] = pl.add("tensor",
                                        (lambda e, m_=m, s_=sq2[m % 2]:
                                         e.matmul(p2_sm[1][0:1, :], ones_b[:], s_[:],
                                                  start=(m_ == 0),
                                                  stop=(m_ == NKT - 1))),
                                        waits=[("s_dv", sdv2)], inc="s_pe")
            st_pe = pl.total("s_pe")
            pl.add("vector",
                   (lambda e: e.scalar_tensor_tensor(out=row_mu[:, 0:512],
                                                     in0=p2_sm[0][0:1, :],
                                                     scalar=INV_D, in1=zrow[:, 0:512],
                                                     op0=ALU.mult, op1=ALU.add)),
                   waits=[("s_pe", st_pe)])
            pl.add("vector",
                   (lambda e: e.scalar_tensor_tensor(out=row_ss[:, 0:512],
                                                     in0=p2_sm[1][0:1, :],
                                                     scalar=INV_D, in1=zrow[:, 0:512],
                                                     op0=ALU.mult, op1=ALU.add)))
            pl.add("vector",
                   (lambda e: e.tensor_tensor(out=row_mu2[:, 0:512], in0=row_mu[:, 0:512],
                                              in1=row_mu[:, 0:512], op=ALU.mult)))
            pl.add("vector", lambda e: e.drain())
            v_sm2 = pl.add("vector",
                           (lambda e: e.tensor_tensor(out=row_var[:, 0:512],
                                                      in0=row_ss[:, 0:512],
                                                      in1=row_mu2[:, 0:512],
                                                      op=ALU.subtract)),
                           inc="s_dv")
            sm2_free.clear()
            sm2_free.append(("s_dv", v_sm2))
            pl.add("scalar",
                   (lambda e: e.activation(out=row_lnv[:, 0:512], in_=row_var[:, 0:512],
                                           func=AF.Ln, bias=ceps[0:1, 0:1])),
                   waits=[("s_dv", v_sm2)])
            a_r2 = pl.add("scalar",
                          (lambda e: e.activation(out=row_rstd[:, 0:512],
                                                  in_=row_lnv[:, 0:512],
                                                  func=AF.Exp, scale=-0.5)),
                          inc="s_ac")
            pl.add("vector",
                   (lambda e: e.tensor_tensor(out=row_q[:, 0:512], in0=row_mu[:, 0:512],
                                              in1=row_rstd[:, 0:512], op=ALU.mult)),
                   waits=[("s_ac", a_r2)])
            pl.add("vector", lambda e: e.drain())
            pl.add("vector",
                   (lambda e: e.tensor_tensor(out=row_r_bf[:, 0:512],
                                              in0=row_rstd[:, 0:512],
                                              in1=zrow[:, 0:512], op=ALU.add)))
            pl.add("vector",
                   (lambda e: e.tensor_tensor(out=row_q_bf[:, 0:512],
                                              in0=row_q[:, 0:512],
                                              in1=zrow[:, 0:512], op=ALU.add)))
            rows_dv[c] = pl.add("vector",
                                (lambda e: e.tensor_tensor(out=settle_b[:],
                                                           in0=settle_b[:],
                                                           in1=settle_b[:],
                                                           op=ALU.add)),
                                inc="s_dv")

        def emit_ffn(c):
            nonlocal l8_free, tp_free, ls_guard, msk_guard, asc_guard, eneg_guard
            par = c % 2
            router_pe = router_pe_d[c]
            v_ls = pl.add("vector",
                          (lambda e: e.tensor_copy(ls_sb[:], p2_l8[:])),
                          waits=[("s_pe", router_pe)] + ls_guard, inc="s_dv")
            l8_free.append(("s_dv", v_ls))
            if debug:
                pl.add("sync", (lambda e, c_=c:
                                e.dma_start(out=taps["t_ls"][c_, :, :], in_=ls_sb[:])),
                       waits=[("s_dv", v_ls)], inc="s_tap", amt=16)
                ls_guard = [("s_tap", pl.total("s_tap"))]
            wts = tp_free
            tp_free = []
            tp_pe = None
            for g in range(4):
                tp_pe = pl.add("tensor",
                               (lambda e, g_=g:
                                e.transpose(p2_tp[:, g_ * 8:(g_ + 1) * 8],
                                            ls_sb[0:8, g_ * 128:(g_ + 1) * 128],
                                            ident_sb[0:8, 0:8])),
                               waits=[("s_dv", v_ls)] + (wts if g == 0 else [])
                                     + ([consts_ld]
                                        if (c == 0 and g == 0) else []),
                               inc="s_pe")
            v_lg = pl.add("vector",
                          (lambda e: e.tensor_copy(lg[:], p2_tp[:])),
                          waits=[("s_pe", tp_pe)], inc="s_dv")
            tp_free.append(("s_dv", v_lg))
            pl.add("vector", lambda e: e.drain())
            lg3 = lg[:].rearrange("p (g e) -> p g e", g=4)
            for d in range(1, 8):
                c3 = cds[d - 1][:].rearrange("p (g e) -> p g e", g=4)
                pl.add("vector",
                       (lambda e, d_=d, c3_=c3:
                        e.tensor_tensor(out=c3_[:, :, 0:8 - d_], in0=lg3[:, :, d_:8],
                                        in1=lg3[:, :, 0:8 - d_], op=ALU.is_gt)))
                pl.add("vector",
                       (lambda e, d_=d, c3_=c3:
                        e.tensor_tensor(out=c3_[:, :, 8 - d_:8], in0=lg3[:, :, 0:d_],
                                        in1=lg3[:, :, 8 - d_:8], op=ALU.is_gt)))
            pl.add("vector",
                   (lambda e: e.tensor_tensor(out=tsc[0][:], in0=cds[0][:],
                                              in1=cds[1][:], op=ALU.add)))
            pl.add("vector",
                   (lambda e: e.tensor_tensor(out=tsc[1][:], in0=cds[2][:],
                                              in1=cds[3][:], op=ALU.add)))
            pl.add("vector",
                   (lambda e: e.tensor_tensor(out=tsc[2][:], in0=cds[4][:],
                                              in1=cds[5][:], op=ALU.add)))
            pl.add("vector",
                   (lambda e: e.tensor_tensor(out=tsc[3][:], in0=tsc[0][:],
                                              in1=tsc[1][:], op=ALU.add)))
            pl.add("vector",
                   (lambda e: e.tensor_tensor(out=tsc[4][:], in0=tsc[2][:],
                                              in1=cds[6][:], op=ALU.add)))
            pl.add("vector", lambda e: e.drain())
            pl.add("vector",
                   (lambda e: e.tensor_tensor(out=masked[:], in0=tsc[3][:],
                                              in1=tsc[4][:], op=ALU.add)))
            pl.add("vector", lambda e: e.drain())
            msk_dv = pl.add("vector",
                            (lambda e: e.scalar_tensor_tensor(out=msk[:], in0=masked[:],
                                                              scalar=1.5, in1=zeros32[:],
                                                              op0=ALU.is_le,
                                                              op1=ALU.add)),
                            inc="s_dv")
            g_pe = None
            for g in range(4):
                g_pe = pl.add("tensor",
                              (lambda e, g_=g:
                               e.transpose(p2_l8[0:8, g_ * 128:(g_ + 1) * 128],
                                           msk[:, g_ * 8:(g_ + 1) * 8], ident_sb[:])),
                              waits=(([("s_dv", msk_dv)] + l8_free + msk_guard)
                                     if g == 0 else []),
                              inc="s_pe")
            l8_free = []
            v_mkem = pl.add("vector",
                            (lambda e: e.tensor_copy(mk_em[:], p2_l8[:])),
                            waits=[("s_pe", g_pe)], inc="s_dv")
            a_eem = pl.add("scalar",
                           (lambda e: e.activation(out=e_em[:], in_=ls_sb[:],
                                                   func=AF.Exp)),
                           waits=[("s_dv", v_ls)], inc="s_ac")
            sp_pe = pl.add("tensor",
                           (lambda e: e.matmul(p2_sm[0][0:1, :], ones8f[:], e_em[:],
                                               start=True, stop=True)),
                           waits=[("s_ac", a_eem)] + sm2_free, inc="s_pe")
            v_srow = pl.add("vector",
                            (lambda e: e.tensor_copy(s_row8[:], p2_sm[0][0:1, :])),
                            waits=[("s_pe", sp_pe)], inc="s_dv")
            sm2_free.clear()
            sm2_free.append(("s_dv", v_srow))
            pl.add("scalar",
                   (lambda e: e.activation(out=row_lnv[:, 0:512], in_=s_row8[:],
                                           func=AF.Ln)),
                   waits=[("s_dv", v_srow)])
            a_rs = pl.add("scalar",
                          (lambda e: e.activation(out=rs_row8[:], in_=row_lnv[:, 0:512],
                                                  func=AF.Exp, scale=-1.0)),
                          inc="s_ac")
            rsb_pe = pl.add("tensor",
                            (lambda e: e.matmul(p2_l8[:], ones1x8[:], rs_row8[:],
                                                start=True, stop=True)),
                            waits=[("s_dv", v_mkem), ("s_ac", a_rs)], inc="s_pe")
            pl.add("vector",
                   (lambda e: e.tensor_tensor(out=e_em[:], in0=e_em[:],
                                              in1=mk_em[:], op=ALU.mult)),
                   waits=[("s_ac", a_eem), ("s_dv", v_mkem), ("s_pe", sp_pe)])
            v_gem = pl.add("vector",
                           (lambda e: e.tensor_tensor(out=g_em[:], in0=e_em[:],
                                                      in1=p2_l8[:], op=ALU.mult)),
                           waits=[("s_pe", rsb_pe)], inc="s_dv")
            l8_free.append(("s_dv", v_gem))
            ls_guard = [("s_ac", a_eem), ("s_pe", tp_pe)]
            msk_guard = []
            if debug:
                pl.add("sync", (lambda e, c_=c:
                                e.dma_start(out=taps["t_gem"][c_, :, :], in_=g_em[:])),
                       waits=[("s_dv", v_gem)], inc="s_tap", amt=16)
                msk_guard = [("s_tap", pl.total("s_tap"))]
            a_en = pl.add("scalar",
                          (lambda e: e.activation(out=eneg[:], in_=p2_big[2][:],
                                                  func=AF.Silu)),
                          waits=[("s_pe", router_pe)] + eneg_guard, inc="s_ac")
            bank23_free[2].append(("s_ac", a_en))
            wts = bank23_free[3]
            bank23_free[3] = []
            b_pe = pl.add("tensor",
                          (lambda e: e.matmul(p2_big[3][:], sel_sb[:], g_em[:],
                                              start=True, stop=True)),
                          waits=[("s_dv", v_gem)] + wts, inc="s_pe")
            v_asc = pl.add("vector",
                           (lambda e: e.tensor_tensor(out=a_sc[:], in0=eneg[:],
                                                      in1=p2_big[3][:], op=ALU.mult)),
                           waits=[("s_pe", b_pe), ("s_ac", a_en)] + asc_guard,
                           inc="s_dv")
            bank23_free[3].append(("s_dv", v_asc))
            eneg_guard = [("s_dv", v_asc)]
            if debug:
                pl.add("sync", (lambda e, c_=c:
                                e.dma_start(out=taps["t_asc"][c_, :, :], in_=a_sc[:])),
                       waits=[("s_dv", v_asc)], inc="s_tap", amt=16)
                asc_guard = [("s_tap", pl.total("s_tap"))]
            for m in range(NKT):
                b_i = 2 + (m % 2)
                b = p2_big[b_i]
                wts = bank23_free[b_i] + ([("s_dv", v_asc)] if m == 0 else [])
                bank23_free[b_i] = []
                up_pe = pl.add("tensor",
                               (lambda e, m_=m, b_=b:
                                e.matmul(b_[:], u_sb[:, m_ * P:(m_ + 1) * P], a_sc[:],
                                         start=True, stop=True)),
                               waits=wts, inc="s_pe")
                v_f = pl.add("vector",
                             (lambda e, m_=m, b_=b, xn_=x_new[par]:
                              e.tensor_tensor(out=xn_[:, m_ * 512:(m_ + 1) * 512],
                                              in0=b_[:],
                                              in1=xn_[:, m_ * 512:(m_ + 1) * 512],
                                              op=ALU.add)),
                             waits=[("s_pe", up_pe)], inc="s_dv")
                bank23_free[b_i].append(("s_dv", v_f))
                stsem = f"s_st{par}"
                pl.add("sync",
                       (lambda e, m_=m, c_=c, xn_=x_new[par]:
                        e.dma_start(out=out_d[m_ * P:(m_ + 1) * P,
                                              c_ * 512:(c_ + 1) * 512],
                                    in_=xn_[:, m_ * 512:(m_ + 1) * 512])),
                       waits=[("s_dv", v_f)], inc=stsem, amt=16)
            prev_store_st[par] = (f"s_st{par}", pl.total(f"s_st{par}"))

        emit_outmm(0)
        for c in range(NCH2):
            emit_stats(c)
            if c + 1 < NCH2:
                emit_lload(c + 1, [("s_pe", om_last_pe[c])])
                emit_outmm(c + 1, bc_of=c)
            else:
                emit_bcast(c)
                emit_bcevac(c)
                emit_ypair(c, 0)
                emit_ypair(c, 1)
                for kt in range(NKT):
                    emit_ylogits(c, kt)
                    if kt + 2 < NKT:
                        emit_ypair(c, kt + 2)
            emit_ffn(c)

        pl.add("gpsimd", (lambda e: e.engine_nop()),
               waits=[("s_st0", pl.total("s_st0")), ("s_st1", pl.total("s_st1")),
                      ("s_tap", pl.total("s_tap"))])

        block = es.enter_context(nc.Block())

        @block.sync
        def _(eng):
            pl.emit("sync", eng)

        @block.scalar
        def _(eng):
            pl.emit("scalar", eng)

        @block.vector
        def _(eng):
            pl.emit("vector", eng)

        @block.tensor
        def _(eng):
            pl.emit("tensor", eng)

        @block.gpsimd
        def _(eng):
            pl.emit("gpsimd", eng)

        p2.close()
    return nc


def _prep_inputs(L0_attn_x, L0_attn_w_hg, L0_attn_w_out, L0_ffn_router_w,
                 L0_ffn_down_w, L0_ffn_up_w):
    x2 = np.asarray(L0_attn_x, dtype=np.float32).reshape(16384, D)
    xT = np.ascontiguousarray(x2.T)
    whg = np.ascontiguousarray(
        np.asarray(L0_attn_w_hg, np.float32).reshape(NKT, P, 48, P)
        .transpose(2, 1, 0, 3).reshape(48, P, NKT * P))
    wout = np.ascontiguousarray(
        np.asarray(L0_attn_w_out, np.float32).reshape(NCT, P, 16, P)
        .transpose(2, 1, 0, 3).reshape(16, P, NCT * P))
    rw = np.ascontiguousarray(
        np.asarray(L0_ffn_router_w, np.float32).T.reshape(NKT, P, 8)
        .transpose(1, 0, 2).reshape(P, NKT * 8))
    dw = np.ascontiguousarray(
        np.asarray(L0_ffn_down_w, np.float32).reshape(P, NKT, P)
        .transpose(2, 1, 0).reshape(P, NKT * P))
    uw = np.ascontiguousarray(
        np.asarray(L0_ffn_up_w, np.float32).transpose(0, 2, 1).reshape(P, 16 * P))
    sel8 = np.kron(np.eye(8, dtype=np.float32), np.ones((1, 16), np.float32))
    ident = np.eye(P, dtype=np.float32)
    in_maps = []
    for c in range(8):
        lo = c * TOK - W
        if lo < 0:
            xs = np.concatenate([np.zeros((D, W), np.float32),
                                 xT[:, 0:(c + 1) * TOK]], axis=1)
        else:
            xs = xT[:, lo:(c + 1) * TOK]
        in_maps.append({
            "x": np.ascontiguousarray(xs),
            "whg": whg, "wout": wout, "rw": rw, "dw": dw, "uw": uw,
            "sel8": sel8, "ident": ident,
        })
    return in_maps


def kernel(L0_attn_x, L0_attn_w_hg, L0_attn_w_out, L0_ffn_router_w,
           L0_ffn_down_w, L0_ffn_up_w):
    in_maps = _prep_inputs(L0_attn_x, L0_attn_w_hg, L0_attn_w_out,
                           L0_ffn_router_w, L0_ffn_down_w, L0_ffn_up_w)
    nc = build_program()
    res = run_bass_kernel_spmd(nc, in_maps, core_ids=list(range(8)))
    parts = [np.asarray(res.results[c]["out"]) for c in range(8)]
    full = np.concatenate(parts, axis=1)
    return np.ascontiguousarray(full.T).reshape(1, 16384, D).astype(np.float32)
